# version 11 (final)
# speedup vs baseline: 1.9956x; 1.0176x over previous
import math
import numpy as np
import ml_dtypes
import concourse.bass as bass
import concourse.mybir as mybir
from concourse.bass_utils import run_bass_kernel_spmd

F32 = mybir.dt.float32
BF16 = mybir.dt.bfloat16
AF = mybir.ActivationFunctionType
ALU = mybir.AluOpType
AXL = mybir.AxisListType

E = 1024
H = 16
DH = 64
I = 2816
IH = I // 2
NCORE = 8
CH = 128
EPS = 1e-5
BLK = 16
TAUS = [0.25, 0.75, 1.25, 1.75, 2.5, 3.5, 5.0]
TAUS2 = [0.0625, 0.5625, 1.5625, 3.0625, 6.25, 12.25, 25.0]
INV6 = float(np.float32(1.0) / np.float32(6.0))
WSH = I // NCORE
WDS = E // NCORE
NS_W = [512, 512, 384]


def _slopes_f32():
    def po2(m):
        start = 2.0 ** (-(2.0 ** -(math.log2(m) - 3)))
        return [start * start ** i for i in range(m)]
    s = np.array(po2(H), dtype=np.float32)
    return s * (1.0 - 0 / (12 - 1) + 1e-05)


def _host_consts():
    s = _slopes_f32()
    j = np.arange(CH, dtype=np.float32)
    diff = j[None, :] - j[:, None]
    m = np.where(diff >= 0, np.exp(-s[:, None, None] * diff[None]), 0.0)
    masks = np.transpose(m.astype(np.float32), (1, 0, 2)).reshape(CH, H * CH)
    lamq = np.zeros((CH, 8 * CH), np.float32)
    for et in range(8):
        for p in range(CH):
            h = 2 * et + p // 64
            lamq[p, et * CH:(et + 1) * CH] = np.exp(-s[h] * (j + 1.0))
    lamk = np.zeros((CH, E), np.float32)
    for h in range(H):
        lamk[:, h * DH:(h + 1) * DH] = np.exp(-s[h] * (CH - 1.0 - j))[:, None]
    bf = ml_dtypes.bfloat16
    return masks.astype(bf), lamq.astype(bf), lamk.astype(bf)


def _host_wful(core, seg_tokens):
    s = _slopes_f32()
    sc, bc = core % 4, core // 4
    w = np.zeros((CH, NCORE * 512), np.float32)
    for r in range(NCORE):
        sr, br = r % 4, r // 4
        if br != bc or sr >= sc:
            continue
        lam = np.exp(-s.astype(np.float64) * float(seg_tokens) * float(sc - sr - 1))
        lam = lam.astype(np.float32)
        for et in range(8):
            for mm in range(2):
                w[64 * mm:64 * mm + 64,
                  r * 512 + et * 64:r * 512 + (et + 1) * 64] = lam[2 * et + mm]
    return w.astype(ml_dtypes.bfloat16)


class Ctr:
    __slots__ = ("h", "n")

    def __init__(self, h):
        self.h = h
        self.n = 0

    def inc(self, instr, k=1):
        instr.then_inc(self.h, k)
        self.n += k
        return self.n


class DmaStream:

    def __init__(self, ctrs):
        self.c = ctrs
        self.i = -1

    def issue(self, eng, fn, fence=True):
        if fence:
            self.i += 1
        sc = self.c[self.i % len(self.c)]
        if fence and sc.n > 0:
            eng.wait_ge(sc.h, sc.n)
        sc.inc(fn(), 16)
        return (sc, sc.n)

    def mark(self):
        sc = self.c[self.i % len(self.c)]
        return (sc, sc.n)

    def totals(self):
        return [(sc, sc.n) for sc in self.c]


def build_program(nt, max_phase=9, p3_cut=None):
    SEG = nt * CH
    NSC = nt // 4
    NQ = max(1, SEG // 512)
    assert nt % 4 == 0
    lam_c = [float(np.float32(np.exp(np.float32(-s * CH)))) for s in _slopes_f32()]

    nc = bass.Bass("TRN2", target_bir_lowering=False, debug=False)

    x_in = nc.dram_tensor("x", [SEG, E], F32, kind="ExternalInput").ap()
    wq_in = nc.dram_tensor("wq", [E, E], F32, kind="ExternalInput").ap()
    wk_in = nc.dram_tensor("wk", [E, E], F32, kind="ExternalInput").ap()
    wv_in = nc.dram_tensor("wv", [E, E], F32, kind="ExternalInput").ap()
    wo_in = nc.dram_tensor("wo", [E, E], F32, kind="ExternalInput").ap()
    wg1_in = nc.dram_tensor("wg1", [E, DH], F32, kind="ExternalInput").ap()
    wg2_in = nc.dram_tensor("wg2", [DH, E], F32, kind="ExternalInput").ap()
    wgs_in = nc.dram_tensor("wgs", [WSH, E], F32, kind="ExternalInput").ap()
    wus_in = nc.dram_tensor("wus", [WSH, E], F32, kind="ExternalInput").ap()
    wds_in = nc.dram_tensor("wds", [WDS, I], F32, kind="ExternalInput").ap()
    gam_in = nc.dram_tensor("gam", [CH, E], F32, kind="ExternalInput").ap()
    bet_in = nc.dram_tensor("bet", [CH, E], F32, kind="ExternalInput").ap()
    id_in = nc.dram_tensor("ident", [CH, CH], BF16, kind="ExternalInput").ap()
    mask_in = nc.dram_tensor("masks", [CH, H * CH], BF16, kind="ExternalInput").ap()
    lamq_in = nc.dram_tensor("lamq", [CH, 8 * CH], BF16, kind="ExternalInput").ap()
    lamk_in = nc.dram_tensor("lamk", [CH, E], BF16, kind="ExternalInput").ap()
    wful_in = nc.dram_tensor("wful", [CH, NCORE * 512], BF16, kind="ExternalInput").ap()
    out_d = nc.dram_tensor("out", [SEG, E], F32, kind="ExternalOutput").ap()
    hq_d = nc.dram_tensor("hq_d", [nt, CH, I], BF16).ap()
    op_d = nc.dram_tensor("op_d", [nt, CH, E], BF16).ap()
    g2_d = nc.dram_tensor("g2_d", [nt, CH, E], BF16).ap()
    agg_i = nc.dram_tensor("agg_i", [E, WSH], BF16).ap()
    agg_o = nc.dram_tensor("agg_o", [NCORE, E, WSH], BF16).ap()
    agu_i = nc.dram_tensor("agu_i", [E, WSH], BF16).ap()
    agu_o = nc.dram_tensor("agu_o", [NCORE, E, WSH], BF16).ap()
    agd_i = nc.dram_tensor("agd_i", [I, WDS], BF16).ap()
    agd_o = nc.dram_tensor("agd_o", [NCORE, I, WDS], BF16).ap()
    agt_i = nc.dram_tensor("agt_i", [CH, 512], F32).ap()
    agt_o = nc.dram_tensor("agt_o", [NCORE, CH, 512], F32).ap()

    RG = [list(range(NCORE))]
    Q1C = max(8 * SEG, 8 * IH + 2 * IH)
    Q3C = max(nt, 16) * E
    XNC = max(8 * SEG, 8 * E + 4 * E)

    import contextlib
    with contextlib.ExitStack() as _es:
        Q1 = _es.enter_context(nc.sbuf_tensor("Q1", [CH, Q1C], BF16))
        Q2 = _es.enter_context(nc.sbuf_tensor("Q2", [CH, 8 * IH], BF16))
        Q3 = _es.enter_context(nc.sbuf_tensor("Q3", [CH, Q3C], BF16))
        XN = _es.enter_context(nc.sbuf_tensor("XN", [CH, XNC], BF16))
        Cf = _es.enter_context(nc.sbuf_tensor("Cf", [CH, 9216], F32))
        Db = _es.enter_context(nc.sbuf_tensor("Db", [CH, 6 * IH], BF16))
        maskb = _es.enter_context(nc.sbuf_tensor("maskS", [CH, H * CH], BF16))
        lamqb = _es.enter_context(nc.sbuf_tensor("lamqS", [CH, 8 * CH], BF16))
        lamkb = _es.enter_context(nc.sbuf_tensor("lamkS", [CH, E], BF16))
        idn = _es.enter_context(nc.sbuf_tensor("idn", [CH, CH], BF16))
        wmb = _es.enter_context(nc.sbuf_tensor("wm", [CH, 2 * E], BF16))
        kvacc = _es.enter_context(nc.sbuf_tensor("kvacc", [CH, 512], F32))
        kvbf = _es.enter_context(nc.sbuf_tensor("kvbf", [CH, 512], BF16))
        sjb = _es.enter_context(nc.sbuf_tensor("sjb", [CH, 512], BF16))
        kvbf2scr = _es.enter_context(nc.sbuf_tensor("oiev", [CH, E], BF16))
        sin2 = _es.enter_context(nc.sbuf_tensor("sin2", [CH, 512], F32))
        sj2 = _es.enter_context(nc.sbuf_tensor("sj2", [CH, 512], F32))
        osb = _es.enter_context(nc.sbuf_tensor("osb", [CH, 2 * E], BF16))
        g1sb = _es.enter_context(nc.sbuf_tensor("g1sb", [CH, nt * DH], BF16))
        g1T = _es.enter_context(nc.sbuf_tensor("g1T", [DH, nt * CH], BF16))
        wg1b = _es.enter_context(nc.sbuf_tensor("wg1b", [CH, 8 * DH], BF16))
        wg2b = _es.enter_context(nc.sbuf_tensor("wg2b", [DH, E], BF16))
        stb = _es.enter_context(nc.sbuf_tensor("st", [CH, 24], F32))
        ps_tr = _es.enter_context(nc.psum_tensor("ps_tr", [CH, IH], BF16))
        ps_sc = _es.enter_context(nc.psum_tensor("ps_sc", [CH, E], F32))
        ps_A = _es.enter_context(nc.psum_tensor("ps_A", [CH, 512], F32))
        ps_o = _es.enter_context(nc.psum_tensor("ps_o", [CH, E], F32))
        ps_X = _es.enter_context(nc.psum_tensor("ps_X", [CH, 512], F32))
        _cc = _es.enter_context(nc.semaphore("ccw"))
        _cc2 = _es.enter_context(nc.semaphore("cct"))
        _tc = _es.enter_context(nc.semaphore("tc"))
        _vc = _es.enter_context(nc.semaphore("vc"))
        _sc = _es.enter_context(nc.semaphore("sc"))
        _gc = _es.enter_context(nc.semaphore("gc"))

        def mkstream(name, k):
            return DmaStream([Ctr(_es.enter_context(nc.semaphore(f"{name}{i}")))
                              for i in range(k)])

        S_cst = mkstream("Scst", 1)
        S_sh = mkstream("Ssh", 2)
        S_wg = mkstream("Swg", 1)
        S_wu = mkstream("Swu", 1)
        S_x = mkstream("Sx", 2)
        S_osp = mkstream("Sosp", 2)
        S_wd = mkstream("Swd", 1)
        S_hq = mkstream("Shq", 2)
        S_p5 = mkstream("Sp5", 2)
        S_tld = mkstream("Stld", 2)
        S_p5c = mkstream("Sp5c", 1)
        S_out = mkstream("Sout", 2)
        G_wg12 = mkstream("Gwg12", 1)
        G_bn = mkstream("Gbn", 1)
        G_wk = mkstream("Gwk", 1)
        G_wq = mkstream("Gwq", 2)
        G_wv = mkstream("Gwv", 2)
        G_agt = mkstream("Gagt", 1)
        G_wo = mkstream("Gwo", 1)
        P_sp = mkstream("Psp", 2)
        P_g2 = mkstream("Pg2", 2)
        blk = _es.enter_context(nc.Block())
        cc, cct = Ctr(_cc), Ctr(_cc2)
        tc, vc, scc, gc = Ctr(_tc), Ctr(_vc), Ctr(_sc), Ctr(_gc)

        ALL_STREAMS = [S_cst, S_sh, S_wg, S_wu, S_x, S_osp, S_wd, S_hq, S_p5,
                       S_tld, S_p5c, S_out, G_wg12, G_bn, G_wk, G_wq, G_wv,
                       G_agt, G_wo, P_sp, P_g2]

        def emit_final():
            @blk.sync
            def _(y):
                for st_ in ALL_STREAMS:
                    for scx, nnn in st_.totals():
                        if nnn > 0:
                            y.wait_ge(scx.h, nnn)

        C_x = [Cf[:, 0:E], Cf[:, E:2 * E]]
        C_xn = Cf[:, 2 * E:3 * E]
        C_h = Cf[:, 3 * E:3 * E + I]
        C_t = Cf[:, 3 * E + I:3 * E + 2 * I]
        C_t2 = Cf[:, 3 * E + IH:3 * E + I]
        C_redh = Cf[:, 8704:8704 + 88]
        C_sclh = Cf[:, 8802:8802 + 88]
        C_rech = Cf[:, 8900:8900 + 88]
        C_stg = Cf[:, 0:IH].bitcast(BF16)
        C_wload = [Cf[:, 1408:1408 + E], Cf[:, 1408 + E:1408 + 2 * E]]
        C_kT = Cf[:, 0:2048].bitcast(BF16)
        C_v = Cf[:, 2048:4096].bitcast(BF16)
        C_wd = Cf[:, 0:5632].bitcast(BF16)
        C_hqld = [Cf[:, 5632 + k * 1408:5632 + (k + 1) * 1408].bitcast(BF16)
                  for k in range(2)]
        C_o = Cf[:, 0:E]
        C_on = Cf[:, E:2 * E]
        C_og = Cf[:, 2 * E:3 * E]
        C_gam = Cf[:, 3 * E:4 * E]
        C_bet = Cf[:, 4 * E:5 * E]
        C_x5 = [Cf[:, 5 * E:6 * E], Cf[:, 6 * E:7 * E]]
        C_wful = Cf[:, 7 * E:9 * E].bitcast(BF16)

        DS = [Db[:, k * IH:(k + 1) * IH] for k in range(6)]
        QOUT = Db[:, 4 * IH:6 * IH]
        D_wk = Db[:, 0:8 * E]

        Q2_ktil = Q2[:, 0:E]
        Q2_qch = Q2[:, 0:E]
        Q2_msc = Q2[:, 2048:2048 + H * CH]
        Q2_ogtok = Q2[:, E:2 * E]
        Q2_op = [Q2[:, 4096 + k * E:4096 + (k + 1) * E] for k in range(2)]
        Q2_g2 = [Q2[:, 6144 + k * E:6144 + (k + 1) * E] for k in range(2)]
        Q2_ogT = Q2[:, 8192:8192 + E]
        Q2_ts = [Q2[:, 9216 + k * E:9216 + (k + 1) * E].bitcast(F32) for k in range(2)]
        Q1_stg = [Q1[:, 8 * IH + k * IH:8 * IH + (k + 1) * IH] for k in range(2)]
        XN_wo = XN[:, 0:8 * E]
        XN_out = [XN[:, 8 * E + k * 2 * E:8 * E + (k + 1) * 2 * E].bitcast(F32)
                  for k in range(2)]

        marks = {}

        def wge(eng, key):
            m = marks.get(key)
            if m is not None and m[1] > 0:
                eng.wait_ge(m[0].h, m[1])

        def setm(key, ctr):
            marks[key] = (ctr, ctr.n)

        def q_core(tag, in_f32, out_bf, p, w, wait_keys):
            nb = w // BLK
            g3 = in_f32.rearrange("p (b k) -> p b k", k=BLK)
            sgn = DS[0][0:p, 0:w]
            b1 = DS[1][0:p, 0:w]
            b2 = DS[2][0:p, 0:w]
            b3 = DS[3][0:p, 0:w]
            tsc = C_t[0:p, 0:w]
            tsq = C_t[0:p, w:2 * w]
            red = C_redh[0:p, 0:nb]
            scl = C_sclh[0:p, 0:nb]
            rec = C_rech[0:p, 0:nb]

            @blk.vector
            def _(v):
                for kk in wait_keys:
                    wge(v, kk)
                wge(v, "qg_last")
                wge(v, "qs_last")
                vc.inc(v.tensor_reduce(red.rearrange("p b -> p b ()"), g3, axis=AXL.X,
                                       op=ALU.max, apply_absolute_value=True))
                v.wait_ge(vc.h, vc.n)
                vc.inc(v.tensor_scalar(scl, red, INV6, 1e-12, ALU.mult, ALU.max))
                v.wait_ge(vc.h, vc.n)
                vc.inc(v.reciprocal(rec, scl))
                v.wait_ge(vc.h, vc.n)
                rb = rec.rearrange("p b -> p b ()").broadcast_to((p, nb, BLK))
                vc.inc(v.tensor_tensor(tsc.rearrange("p (b k) -> p b k", k=BLK),
                                       g3, rb, ALU.mult))
                v.wait_ge(vc.h, vc.n)
                vc.inc(v.tensor_tensor(tsq, tsc, tsc, ALU.mult))
            setm(tag + "_t", vc)

            @blk.vector
            def _(v):
                wge(v, tag + "_t")
                wge(v, "qout_free")
                v.tensor_scalar(b1, tsq, TAUS2[0], None, ALU.is_gt)
                vc.inc(v.tensor_scalar(b2, tsq, TAUS2[1], None, ALU.is_gt))
                v.wait_ge(vc.h, vc.n)
                vc.inc(v.tensor_tensor(b1, b1, b2, ALU.add))
                v.wait_ge(vc.h, vc.n)
                vc.inc(v.tensor_scalar(b2, tsq, TAUS2[2], None, ALU.is_gt))
                v.wait_ge(vc.h, vc.n)
                vc.inc(v.scalar_tensor_tensor(b1, b2, 1.0, b1, ALU.mult, ALU.add))
                v.wait_ge(vc.h, vc.n)
                vc.inc(v.tensor_scalar(b2, tsq, TAUS2[3], None, ALU.is_gt))
                v.wait_ge(vc.h, vc.n)
                vc.inc(v.tensor_tensor(b1, b1, b2, ALU.add))
                v.wait_ge(vc.h, vc.n)
                v.tensor_scalar(DS[4][0:p, 0:w], tsq, TAUS2[4], None, ALU.is_gt)
                vc.inc(v.tensor_scalar(DS[5][0:p, 0:w], tsq, TAUS2[5], None,
                                       ALU.is_gt))
                v.wait_ge(vc.h, vc.n)
                vc.inc(v.tensor_tensor(b2, DS[4][0:p, 0:w], DS[5][0:p, 0:w], ALU.add))
                v.wait_ge(vc.h, vc.n)
                vc.inc(v.scalar_tensor_tensor(b1, b1, 0.5, b2, ALU.mult, ALU.add))
                v.wait_ge(vc.h, vc.n)
                vc.inc(v.tensor_scalar(b3, tsq, TAUS2[6], None, ALU.is_gt))
                v.wait_ge(vc.h, vc.n)
                vc.inc(v.scalar_tensor_tensor(b1, b3, 2.0, b1, ALU.mult, ALU.add))
                v.wait_ge(vc.h, vc.n)
                vc.inc(v.tensor_scalar(sgn, tsc, 0.0, 2.0, ALU.is_gt, ALU.mult))
                v.wait_ge(vc.h, vc.n)
                vc.inc(v.scalar_tensor_tensor(b2, sgn, 1.0, b1, ALU.subtract,
                                              ALU.mult))
                v.wait_ge(vc.h, vc.n)
                sb = scl.rearrange("p b -> p b ()").broadcast_to((p, nb, BLK))
                vc.inc(v.tensor_tensor(out_bf.rearrange("p (b k) -> p b k", k=BLK),
                                       b2.rearrange("p (b k) -> p b k", k=BLK), sb,
                                       ALU.mult))
            setm(tag, vc)
            setm(tag + "_g", vc)

        @blk.sync
        def _(y):
            S_cst.issue(y, lambda: y.dma_start(out=idn[:, :], in_=id_in))
            S_cst.issue(y, lambda: y.dma_start(out=maskb[:, :], in_=mask_in), fence=False)
            S_cst.issue(y, lambda: y.dma_start(out=lamqb[:, :], in_=lamq_in), fence=False)
            S_cst.issue(y, lambda: y.dma_start(out=lamkb[:, :], in_=lamk_in), fence=False)
        marks["consts"] = S_cst.mark()

        @blk.gpsimd
        def _(g):
            G_wg12.issue(g, lambda: g.dma_start(
                out=wg1b[:, :].rearrange("p (et d) -> p et d", d=DH),
                in_=wg1_in.rearrange("(et p) d -> p et d", p=CH)))
            G_wg12.issue(g, lambda: g.dma_start(out=wg2b[:, :], in_=wg2_in), fence=False)
        marks["wg12"] = G_wg12.mark()

        for wi, (wname, src, agi) in enumerate((("sg", wgs_in, agg_i),
                                                ("su", wus_in, agu_i))):
            for k, (o, p) in enumerate(((0, 128), (128, 128), (256, 96))):
                gi = wi * 3 + k
                slot = C_wload[gi % 2]

                @blk.sync
                def _(y, o=o, p=p, slot=slot, k=k, gi=gi, wname=wname):
                    wge(y, "consts")
                    wge(y, f"shfree{gi - 2}")
                    marks[f"{wname}_ld{k}"] = S_sh.issue(y, lambda: y.dma_start(out=slot[0:p, 0:E], in_=src[o:o + p, :]))

                q_core(f"{wname}q{k}", slot[0:p, 0:E], QOUT[0:p, 0:E], p, E,
                       (f"{wname}_ld{k}",))
                setm(f"shfree{gi}", vc)

                @blk.tensor
                def _(t_, k=k, p=p, wname=wname):
                    wge(t_, f"{wname}q{k}")
                    wge(t_, "ps_tr_free")
                    for et in range(8):
                        ins = t_.transpose(ps_tr[:, et * CH:et * CH + p],
                                           QOUT[0:p, et * CH:(et + 1) * CH],
                                           idn[0:p, 0:p])
                    tc.inc(ins)
                setm(f"{wname}_tr{k}", tc)
                setm("qout_free", tc)

                @blk.scalar
                def _(s, k=k, o=o, p=p, wname=wname):
                    wge(s, f"{wname}_tr{k}")
                    wge(s, f"{wname}_stgfree")
                    scc.inc(s.copy(
                        C_stg[:, :].rearrange("p (et r) -> p et r", r=WSH)[:, :, o:o + p],
                        ps_tr[:, 0:8 * CH].rearrange("p (et r) -> p et r", r=CH)
                        [:, :, 0:p]))
                setm("ps_tr_free", scc)
                setm(f"{wname}_evac{k}", scc)

            @blk.gpsimd
            def _(g, wname=wname, agi=agi):
                wge(g, f"{wname}_evac2")
                marks[f"{wname}_bounce"] = G_bn.issue(g, lambda: g.dma_start(out=agi.rearrange("(et p) r -> p et r", p=CH),
                                   in_=C_stg[:, :].rearrange("p (et r) -> p et r",
                                                             r=WSH)))

            @blk.scalar
            def _(s, wname=wname):
                wge(s, f"{wname}_bounce")
                scc.inc(s.memzero(stb[:, 22:23]))
            nxt = {"sg": "su_stgfree", "su": "sd_stgfree"}[wname]
            setm(nxt, scc)

        for hh in range(2):
            @blk.sync
            def _(y, hh=hh):
                wge(y, "consts")
                wge(y, "shfree5")
                marks[f"sd_ld{hh}"] = S_sh.issue(
                    y, lambda: y.dma_start(out=C_h[0:WDS, hh * IH:(hh + 1) * IH],
                                           in_=wds_in[:, hh * IH:(hh + 1) * IH]))

            q_core(f"sdq{hh}", C_h[0:WDS, hh * IH:(hh + 1) * IH],
                   QOUT[0:WDS, 0:IH], WDS, IH, (f"sd_ld{hh}",))

            @blk.tensor
            def _(t_, hh=hh):
                wge(t_, f"sdq{hh}")
                wge(t_, "ps_tr_free")
                for il in range(11):
                    ins = t_.transpose(ps_tr[:, il * CH:(il + 1) * CH],
                                       QOUT[0:WDS, il * CH:(il + 1) * CH],
                                       idn[0:WDS, 0:WDS])
                tc.inc(ins)
            setm(f"sd_tr{hh}", tc)
            setm("qout_free", tc)

            @blk.scalar
            def _(s, hh=hh):
                wge(s, f"sd_tr{hh}")
                if hh == 0:
                    wge(s, "sd_stgfree")
                scc.inc(s.copy(C_stg[:, hh * 11 * CH:(hh + 1) * 11 * CH],
                               ps_tr[:, 0:11 * CH]))
            setm("ps_tr_free", scc)
            setm(f"sd_evac{hh}", scc)

        @blk.gpsimd
        def _(g):
            wge(g, "sd_evac1")
            mk = G_bn.issue(g, lambda: g.dma_start(out=agd_i.rearrange("(it p) r -> p it r", p=CH),
                               in_=C_stg[:, :].rearrange("p (it r) -> p it r", r=CH)))
            marks["sd_bounce"] = mk
            g.wait_ge(mk[0].h, mk[1])
            cc.inc(g.collective_compute("AllGather", ALU.bypass, replica_groups=RG,
                                        ins=[agg_i], outs=[agg_o]))
            cc.inc(g.collective_compute("AllGather", ALU.bypass, replica_groups=RG,
                                        ins=[agu_i], outs=[agu_o]))
            cc.inc(g.collective_compute("AllGather", ALU.bypass, replica_groups=RG,
                                        ins=[agd_i], outs=[agd_o]))
        setm("w_bounced", cc)
        setm("ag_w", cc)
        setm("p0_v", vc)

        if max_phase <= 1:
            emit_final()
            return nc

        def load_whalf(tag, ag_out, dst, ranks, freekey, stream):
            @blk.sync
            def _(y):
                wge(y, "ag_w")
                wge(y, freekey)
                first = True
                for et in range(8):
                    for r in ranks:
                        stream.issue(y, lambda: y.dma_start(
                            out=dst[:, et * IH + (r % 4) * WSH:
                                    et * IH + (r % 4 + 1) * WSH],
                            in_=ag_out[r, et * CH:(et + 1) * CH, :]), fence=first)
                        first = False
            marks[tag] = stream.mark()

        for t in range(nt):
            xs = C_x[t % 2]

            @blk.sync
            def _(y, t=t, xs=xs):
                wge(y, f"xnfree{t - 2}")
                if t < 2:
                    wge(y, "p0_v")
                    wge(y, "w_bounced")
                marks[f"xld{t}"] = S_x.issue(y, lambda: y.dma_start(out=xs, in_=x_in[t * CH:(t + 1) * CH, :]))

            @blk.vector
            def _(v, t=t, xs=xs):
                wge(v, f"xld{t}")
                vc.inc(v.bn_stats(stb[:, 0:6], xs[:, 0:512]))
                vc.inc(v.bn_stats(stb[:, 6:12], xs[:, 512:1024]))
                v.wait_ge(vc.h, vc.n)
                vc.inc(v.bn_aggr(stb[:, 12:14], stb[:, 0:12]))
                v.wait_ge(vc.h, vc.n)
                vc.inc(v.tensor_scalar(stb[:, 14:15], stb[:, 13:14], EPS, None,
                                       ALU.add))
            setm(f"var{t}", vc)

            @blk.scalar
            def _(s, t=t):
                wge(s, f"var{t}")
                scc.inc(s.activation(stb[:, 15:16], stb[:, 14:15], AF.Sqrt))
            setm(f"sqr{t}", scc)

            @blk.vector
            def _(v, t=t, xs=xs):
                wge(v, f"sqr{t}")
                vc.inc(v.reciprocal(stb[:, 16:17], stb[:, 15:16]))
                v.wait_ge(vc.h, vc.n)
                wge(v, f"xnT_tr{t - 1}")
                v.tensor_scalar(C_xn[:, :], xs, stb[:, 12:13], stb[:, 16:17],
                                ALU.subtract, ALU.mult)
                vc.inc(v.tensor_scalar(DS[1][:, 0:E], xs, stb[:, 12:13],
                                       stb[:, 16:17], ALU.subtract, ALU.mult))
            setm(f"xn{t}", vc)
            setm(f"xnfree{t}", vc)

            @blk.tensor
            def _(t_, t=t):
                wge(t_, f"xn{t}")
                wge(t_, "ps_tr_free")
                for et in range(8):
                    ins = t_.transpose(ps_tr[:, et * CH:(et + 1) * CH],
                                       DS[1][:, et * CH:(et + 1) * CH], idn[:, :])
                tc.inc(ins)
            setm(f"xnT_tr{t}", tc)

            @blk.scalar
            def _(s, t=t):
                wge(s, f"xnT_tr{t}")
                scc.inc(s.copy(
                    XN[:, 0:8 * SEG].rearrange("p (et s) -> p et s", s=SEG)
                    [:, :, t * CH:(t + 1) * CH],
                    ps_tr[:, 0:8 * CH].rearrange("p (et s) -> p et s", s=CH)))
            setm("ps_tr_free", scc)
            setm(f"xnT{t}", scc)

            q_core(f"xq{t}", C_xn[:, :], DS[1][:, 0:E], CH, E,
                   (f"xn{t}", f"xnT_tr{t}"))

            @blk.tensor
            def _(t_, t=t):
                wge(t_, f"xq{t}")
                wge(t_, "ps_tr_free")
                for et in range(8):
                    ins = t_.transpose(ps_tr[:, et * CH:(et + 1) * CH],
                                       DS[1][:, et * CH:(et + 1) * CH], idn[:, :])
                tc.inc(ins)
            setm(f"xqT_tr{t}", tc)

            @blk.scalar
            def _(s, t=t):
                wge(s, f"xqT_tr{t}")
                scc.inc(s.copy(
                    Q3[:, t * E:(t + 1) * E].rearrange("p (et s) -> p et s",
                                                       s=CH),
                    ps_tr[:, 0:8 * CH].rearrange("p (et s) -> p et s", s=CH)))
            setm("ps_tr_free", scc)
            setm(f"xqT{t}", scc)

        pending_evac = []
        pending_tr = []

        def flush_evac():
            while pending_tr:
                pending_tr.pop(0)()
            while pending_evac:
                pending_evac.pop(0)()

        for ih in range(2):
            ranks = list(range(4 * ih, 4 * ih + 4))
            load_whalf(f"wg_h{ih}", agg_o, Q2[:, 0:8 * IH], ranks,
                       "ffn_done_ih0" if ih == 1 else "_none", S_wg)
            load_whalf(f"wu_h{ih}", agu_o, Q1[:, 0:8 * IH], ranks,
                       "ffn_done_ih0" if ih == 1 else "_none", S_wu)


            for t in range(nt):
                for ns, w in enumerate(NS_W):
                    off = sum(NS_W[:ns])
                    gidx = (ih * nt + t) * 3 + ns
                    pg, pu = [(ps_sc[:, 0:512], ps_sc[:, 512:1024]),
                              (ps_o[:, 0:512], ps_o[:, 512:1024]),
                              (ps_X[:, 0:512], ps_A[:, 0:512])][gidx % 3]

                    @blk.tensor
                    def _(t_, t=t, ih=ih, w=w, off=off, gidx=gidx, pg=pg, pu=pu):
                        wge(t_, f"xqT{t}")
                        wge(t_, f"wg_h{ih}")
                        wge(t_, f"wu_h{ih}")
                        wge(t_, f"ffn_su_g{gidx - 3}")
                        for et in range(8):
                            ins = t_.matmul(
                                pg[:, 0:w],
                                Q3[:, t * E + et * CH:t * E + (et + 1) * CH],
                                Q2[:, et * IH + off:et * IH + off + w],
                                start=(et == 0), stop=(et == 7))
                        tc.inc(ins)
                        marks[f"ffn_g_g{gidx}"] = (tc, tc.n)
                        for et in range(8):
                            ins = t_.matmul(
                                pu[:, 0:w],
                                Q3[:, t * E + et * CH:t * E + (et + 1) * CH],
                                Q1[:, et * IH + off:et * IH + off + w],
                                start=(et == 0), stop=(et == 7))
                        tc.inc(ins)
                        marks[f"ffn_u_g{gidx}"] = (tc, tc.n)

                    @blk.scalar
                    def _(s, w=w, off=off, gidx=gidx, pg=pg):
                        m = marks[f"ffn_g_g{gidx}"]
                        s.wait_ge(m[0].h, m[1])
                        wge(s, f"ffn_su_g{gidx - 3}")
                        scc.inc(s.activation(C_t2[:, off:off + w], pg[:, 0:w],
                                             AF.Silu))
                    setm(f"ffn_sg_g{gidx}", scc)

                    @blk.vector
                    def _(v, w=w, off=off, gidx=gidx, pu=pu):
                        wge(v, f"ffn_sg_g{gidx}")
                        m = marks[f"ffn_u_g{gidx}"]
                        v.wait_ge(m[0].h, m[1])
                        vc.inc(v.tensor_tensor(C_h[:, off:off + w],
                                               C_t2[:, off:off + w],
                                               pu[:, 0:w], ALU.mult))
                    setm(f"ffn_su_g{gidx}", vc)

                flush_evac()

                q_core(f"hq{t}_{ih}", C_h[:, 0:IH], QOUT[:, 0:IH], CH, IH,
                       (f"ffn_su_g{(ih * nt + t) * 3 + 2}",))

                def mk_tr(t=t, ih=ih):
                    @blk.tensor
                    def _(t_, t=t, ih=ih):
                        wge(t_, f"hq{t}_{ih}")
                        wge(t_, "ps_tr_free")
                        for it in range(11):
                            ins = t_.transpose(ps_tr[:, it * CH:(it + 1) * CH],
                                               QOUT[:, it * CH:(it + 1) * CH],
                                               idn[:, :])
                        tc.inc(ins)
                    setm(f"hqT_tr{t}_{ih}", tc)
                    setm("qout_free", tc)
                pending_tr.append(mk_tr)

                def mk_evac(t=t, ih=ih):
                    slot = Q1_stg[(2 * t + ih) % 2]

                    @blk.scalar
                    def _(s, t=t, ih=ih, slot=slot):
                        wge(s, f"hqT_tr{t}_{ih}")
                        wge(s, f"stgfree{2 * t + ih - 2}")
                        scc.inc(s.copy(slot, ps_tr[:, 0:IH]))
                    setm("ps_tr_free", scc)
                    setm(f"hqT_ev{t}_{ih}", scc)

                    @blk.scalar
                    def _(s, t=t, ih=ih, slot=slot):
                        wge(s, f"hqT_ev{t}_{ih}")
                        mk = P_sp.issue(s, lambda: s.dma_start(
                            out=hq_d[t, :, ih * IH:(ih + 1) * IH], in_=slot))
                        marks[f"stgfree{2 * t + ih}"] = mk
                        marks[f"spill{t}_{ih}"] = mk
                pending_evac.append(mk_evac)

            if ih == 0:
                setm("ffn_done_ih0", tc)
        flush_evac()
        setm("p2a_t", tc)
        setm("p2a_v", vc)

        if max_phase <= 2:
            emit_final()
            return nc

        @blk.vector
        def _(v):
            vc.inc(v.memset(kvacc[:, :], 0.0))
        setm("kv0", vc)

        @blk.gpsimd
        def _(g):
            wge(g, "p2a_v")
            wge(g, "qout_free")
            for et in range(8):
                G_wk.issue(g, lambda: g.dma_start(out=D_wk[:, et * E:(et + 1) * E],
                                          in_=wk_in[et * CH:(et + 1) * CH, :]),
                           fence=(et == 0))
        marks["wk_ld"] = G_wk.mark()

        for m in range(8):
            @blk.gpsimd
            def _(g, m=m):
                wge(g, f"wmfree{m - 2}")
                marks[f"wqm{m}"] = G_wq.issue(g, lambda: g.dma_start(
                    out=wmb[:, (m % 2) * E:(m % 2 + 1) * E]
                    .rearrange("p (et d) -> p et d", d=CH),
                    in_=wq_in[:, m * CH:(m + 1) * CH]
                    .rearrange("(et p) d -> p et d", p=CH)))

            for ns in range(NQ):
                g_i = m * NQ + ns
                slot = g_i % 2

                @blk.tensor
                def _(t_, m=m, ns=ns, g_i=g_i, slot=slot):
                    wge(t_, f"wqm{m}")
                    if g_i == 0:
                        wge(t_, "p2a_v")
                    wge(t_, f"qevac_g{g_i - 2}")
                    for et in range(8):
                        ins = t_.matmul(
                            ps_sc[:, slot * 512:slot * 512 + 512],
                            wmb[:, (m % 2) * E + et * CH:(m % 2) * E + (et + 1) * CH],
                            XN[:, et * SEG + ns * 512:et * SEG + (ns + 1) * 512],
                            start=(et == 0), stop=(et == 7))
                    tc.inc(ins)
                setm(f"qmm_g{g_i}", tc)

                @blk.scalar
                def _(s, m=m, ns=ns, g_i=g_i, slot=slot):
                    wge(s, f"qmm_g{g_i}")
                    wge(s, f"spill{nt - 1}_1")
                    scc.inc(s.activation(
                        Q1[:, m * SEG + ns * 512:m * SEG + (ns + 1) * 512],
                        ps_sc[:, slot * 512:slot * 512 + 512], AF.Silu))
                setm(f"qevac_g{g_i}", scc)
            setm(f"wmfree{m}", scc)
        setm("qT_done", scc)

        if p3_cut == "q":
            emit_final()
            return nc

        for sc_i in range(NSC):
            for m in range(8):
                @blk.tensor
                def _(t_, sc_i=sc_i, m=m):
                    wge(t_, "wk_ld")
                    if m == 0:
                        wge(t_, "qT_done")
                        wge(t_, f"sc_cons{sc_i - 1}")
                        wge(t_, f"oev{sc_i * 4 - 1}")
                    wge(t_, f"kevac{sc_i}_{m - 2}")
                    for et in range(8):
                        ins = t_.matmul(
                            ps_sc[:, (m % 2) * 512:(m % 2) * 512 + 512],
                            D_wk[:, et * E + m * CH:et * E + (m + 1) * CH],
                            XN[:, et * SEG + sc_i * 512:et * SEG + (sc_i + 1) * 512],
                            start=(et == 0), stop=(et == 7))
                    tc.inc(ins)
                    marks[f"kmm{sc_i}_{m}"] = (tc, tc.n)

                @blk.scalar
                def _(s, sc_i=sc_i, m=m):
                    mm = marks[f"kmm{sc_i}_{m}"]
                    s.wait_ge(mm[0].h, mm[1])
                    scc.inc(s.activation(C_kT[:, m * 512:(m + 1) * 512],
                                         ps_sc[:, (m % 2) * 512:(m % 2) * 512 + 512],
                                         AF.Silu))
                setm(f"kevac{sc_i}_{m}", scc)
            if p3_cut == "k" and sc_i == 0:
                emit_final()
                return nc

            for tl in range(4):
                gt = sc_i * 4 + tl
                for et in range(8):
                    @blk.gpsimd
                    def _(g, et=et, gt=gt):
                        wge(g, f"vwfree_g{gt * 8 + et - 2}")
                        if gt == 0 and et < 2:
                            wge(g, "wmfree7")
                        marks[f"wv{gt}_{et}"] = G_wv.issue(g, lambda: g.dma_start(
                            out=wmb[:, (et % 2) * E:(et % 2 + 1) * E],
                            in_=wv_in[et * CH:(et + 1) * CH, :]))

                    @blk.tensor
                    def _(t_, sc_i=sc_i, tl=tl, et=et, gt=gt):
                        wge(t_, f"wv{gt}_{et}")
                        if et == 0:
                            wge(t_, f"kevac{sc_i}_7")
                            wge(t_, f"vcons{sc_i - 1}")
                            wge(t_, f"oev{sc_i * 4 - 1}")
                            if tl > 0:
                                wge(t_, f"vevac{gt - 1}")
                        t_.matmul(ps_o[:, 0:512],
                                  XN[:, et * SEG + gt * CH:et * SEG + (gt + 1) * CH],
                                  wmb[:, (et % 2) * E:(et % 2) * E + 512],
                                  start=(et == 0), stop=(et == 7))
                        ins = t_.matmul(ps_o[:, 512:1024],
                                        XN[:, et * SEG + gt * CH:et * SEG + (gt + 1) * CH],
                                        wmb[:, (et % 2) * E + 512:(et % 2) * E + E],
                                        start=(et == 0), stop=(et == 7))
                        tc.inc(ins)
                    setm(f"vmm{gt}_{et}", tc)
                    setm(f"vwfree_g{gt * 8 + et}", tc)

                @blk.scalar
                def _(s, tl=tl, gt=gt):
                    m = marks[f"vmm{gt}_7"]
                    s.wait_ge(m[0].h, m[1])
                    scc.inc(s.activation(C_v[:, tl * E:(tl + 1) * E], ps_o[:, :],
                                         AF.Silu))
                setm(f"vevac{gt}", scc)
            if p3_cut == "v" and sc_i == 0:
                emit_final()
                return nc

            for j in range(4):
                jc = sc_i * 4 + j

                @blk.tensor
                def _(t_, sc_i=sc_i, j=j, jc=jc):
                    wge(t_, f"kevac{sc_i}_7")
                    wge(t_, "ps_tr_free")
                    for m in range(8):
                        ins = t_.transpose(
                            ps_tr[:, m * CH:(m + 1) * CH],
                            C_kT[:, m * 512 + j * CH:m * 512 + (j + 1) * CH], idn[:, :])
                    tc.inc(ins)
                setm(f"kt_tr{jc}", tc)

                @blk.vector
                def _(v, jc=jc):
                    wge(v, f"kt_tr{jc}")
                    wge(v, "consts")
                    vc.inc(v.tensor_tensor(Q2_ktil, ps_tr[:, 0:E], lamkb[:, :],
                                           ALU.mult))
                    wge(v, "kv0" if jc == 0 else f"scan{jc - 1}")
                    v.wait_ge(vc.h, vc.n)
                    vc.inc(v.tensor_copy(kvbf[:, :], kvacc[:, :]))
                setm(f"ktil{jc}", vc)
                setm("ps_tr_free", vc)
                if p3_cut == f"ktil{jc}":
                    emit_final()
                    return nc

                @blk.tensor
                def _(t_, jc=jc, j=j):
                    wge(t_, f"ktil{jc}")
                    wge(t_, f"scan{jc - 1}")
                    for par in range(2):
                        for h in [hh for hh in range(H) if hh % 2 == par]:
                            et, mm_ = h // 2, h % 2
                            ins = t_.matmul(
                                ps_A[64 * mm_:64 * mm_ + 64, et * 64:(et + 1) * 64],
                                Q2_ktil[:, h * DH:(h + 1) * DH],
                                C_v[:, j * E + h * DH:j * E + (h + 1) * DH],
                                start=True, stop=True, tile_position=(0, 64 * mm_))
                        tc.inc(ins)
                        t_.wait_ge(tc.h, tc.n)
                setm(f"Amm{jc}", tc)
                if p3_cut == f"amm{jc}":
                    emit_final()
                    return nc

                for grp in range(2):
                    @blk.tensor
                    def _(t_, jc=jc, j=j, grp=grp, sc_i=sc_i):
                        if grp == 0:
                            wge(t_, "qT_done")
                            wge(t_, f"kevac{sc_i}_7")
                            wge(t_, f"msc{jc - 1}_1")
                            wge(t_, f"oev{jc - 1}")
                        else:
                            wge(t_, f"msc{jc}_0")
                        for par in range(2):
                            for hh in ([0, 2, 4, 6] if par == 0 else [1, 3, 5, 7]):
                                h = grp * 8 + hh
                                et, mm_ = h // 2, h % 2
                                ins = t_.matmul(
                                    ps_sc[:, hh * CH:(hh + 1) * CH],
                                    C_kT[64 * mm_:64 * mm_ + 64,
                                         et * 512 + j * CH:et * 512 + (j + 1) * CH],
                                    Q1[64 * mm_:64 * mm_ + 64,
                                       et * SEG + jc * CH:et * SEG + (jc + 1) * CH],
                                    start=True, stop=True)
                            tc.inc(ins)
                            t_.wait_ge(tc.h, tc.n)
                    setm(f"scmm{jc}_{grp}", tc)
                    if p3_cut == f"scmm{jc}_{grp}":
                        @blk.vector
                        def _(v, jc=jc, grp=grp):
                            wge(v, f"scmm{jc}_{grp}")
                            vc.inc(v.tensor_copy(kvbf2scr[:, :], ps_sc[:, :]))
                        emit_final()
                        return nc

                    @blk.vector
                    def _(v, jc=jc, grp=grp):
                        wge(v, f"scmm{jc}_{grp}")
                        wge(v, f"omm{jc - 1}")
                        vc.inc(v.tensor_tensor(
                            Q2_msc[:, grp * 1024:(grp + 1) * 1024], ps_sc[:, :],
                            maskb[:, grp * 1024:(grp + 1) * 1024], ALU.mult))
                    setm(f"msc{jc}_{grp}", vc)
                    if p3_cut == f"msc{jc}_{grp}":
                        emit_final()
                        return nc

                @blk.vector
                def _(v, jc=jc):
                    wge(v, f"msc{jc}_1")
                    for et in range(8):
                        ins = v.tensor_tensor(
                            Q2_qch[:, et * CH:(et + 1) * CH],
                            Q1[:, et * SEG + jc * CH:et * SEG + (jc + 1) * CH],
                            lamqb[:, et * CH:(et + 1) * CH], ALU.mult)
                    vc.inc(ins)
                setm(f"qch{jc}", vc)

                @blk.tensor
                def _(t_, jc=jc, j=j, sc_i=sc_i):
                    wge(t_, f"qch{jc}")
                    wge(t_, f"oev{jc - 1}")
                    if j == 0:
                        wge(t_, f"vevac{sc_i * 4 + 3}")
                    for h in range(H):
                        ins = t_.matmul(ps_o[:, h * DH:(h + 1) * DH],
                                        Q2_msc[:, h * CH:(h + 1) * CH],
                                        C_v[:, j * E + h * DH:j * E + (h + 1) * DH],
                                        start=True, stop=True)
                    tc.inc(ins)
                    t_.wait_ge(tc.h, tc.n)
                    for par in range(2):
                        for h in [hh for hh in range(H) if hh % 2 == par]:
                            et, mm_ = h // 2, h % 2
                            ins = t_.matmul(ps_sc[:, h * DH:(h + 1) * DH],
                                            Q2_qch[64 * mm_:64 * mm_ + 64,
                                                   et * CH:(et + 1) * CH],
                                            kvbf[64 * mm_:64 * mm_ + 64,
                                                 et * 64:(et + 1) * 64],
                                            start=True, stop=True)
                        tc.inc(ins)
                        t_.wait_ge(tc.h, tc.n)
                setm(f"omm{jc}", tc)
                if p3_cut == f"omm{jc}":
                    emit_final()
                    return nc

                @blk.vector
                def _(v, jc=jc):
                    wge(v, f"Amm{jc}")
                    wge(v, f"omm{jc}")
                    for h in range(H):
                        et, mm_ = h // 2, h % 2
                        ins = v.scalar_tensor_tensor(
                            kvacc[64 * mm_:64 * mm_ + 64, et * 64:(et + 1) * 64],
                            kvacc[64 * mm_:64 * mm_ + 64, et * 64:(et + 1) * 64],
                            lam_c[h],
                            ps_A[64 * mm_:64 * mm_ + 64, et * 64:(et + 1) * 64],
                            ALU.mult, ALU.add)
                    vc.inc(ins)
                setm(f"scan{jc}", vc)
                if p3_cut == f"scan{jc}":
                    emit_final()
                    return nc

                @blk.vector
                def _(v, jc=jc):
                    wge(v, f"omm{jc}")
                    wge(v, f"ospill{jc - 2}")
                    vc.inc(v.tensor_copy(kvbf2scr[:, :], ps_sc[:, :]))
                    v.wait_ge(vc.h, vc.n)
                    vc.inc(v.tensor_tensor(osb[:, (jc % 2) * E:(jc % 2 + 1) * E],
                                           ps_o[:, :], kvbf2scr[:, :], ALU.add))
                setm(f"oev{jc}", vc)

                @blk.sync
                def _(y, jc=jc):
                    wge(y, f"oev{jc}")
                    marks[f"ospill{jc}"] = S_osp.issue(y, lambda: y.dma_start(
                        out=op_d[jc], in_=osb[:, (jc % 2) * E:(jc % 2 + 1) * E]))

            setm(f"sc_cons{sc_i}", tc)
            setm(f"vcons{sc_i}", tc)

        @blk.gpsimd
        def _(g):
            wge(g, f"scan{nt - 1}")
            mk = G_agt.issue(g, lambda: g.dma_start(out=agt_i, in_=kvacc[:, :]))
            g.wait_ge(mk[0].h, mk[1])
            cct.inc(g.collective_compute("AllGather", ALU.bypass, replica_groups=RG,
                                         ins=[agt_i], outs=[agt_o]))
        setm("ag_t", cct)

        if max_phase <= 3:
            emit_final()
            return nc

        for t in range(nt):
            @blk.tensor
            def _(t_, t=t):
                if t == 0:
                    wge(t_, f"msc{nt - 1}_1")
                    wge(t_, f"oev{nt - 1}")
                    wge(t_, "wg12")
                wge(t_, f"g1ev{t - 2}")
                for et in range(8):
                    ins = t_.matmul(ps_sc[:, (t % 2) * 512:(t % 2) * 512 + DH],
                                    XN[:, et * SEG + t * CH:et * SEG + (t + 1) * CH],
                                    wg1b[:, et * DH:(et + 1) * DH],
                                    start=(et == 0), stop=(et == 7))
                tc.inc(ins)
            setm(f"g1mm{t}", tc)

            @blk.vector
            def _(v, t=t):
                wge(v, f"g1mm{t}")
                vc.inc(v.tensor_copy(g1sb[:, t * DH:(t + 1) * DH],
                                     ps_sc[:, (t % 2) * 512:(t % 2) * 512 + DH]))
            setm(f"g1ev{t}", vc)

        for bt in range((nt + 7) // 8):
            @blk.tensor
            def _(t_, bt=bt):
                wge(t_, f"g1ev{min(nt - 1, bt * 8 + 7)}")
                wge(t_, "ps_tr_free")
                for k in range(bt * 8, min(nt, bt * 8 + 8)):
                    ins = t_.transpose(ps_tr[0:DH, (k % 8) * CH:(k % 8 + 1) * CH],
                                       g1sb[:, k * DH:(k + 1) * DH], idn[:, :])
                tc.inc(ins)
            setm(f"g1T_tr{bt}", tc)

            @blk.scalar
            def _(s, bt=bt):
                wge(s, f"g1T_tr{bt}")
                n = min(nt, bt * 8 + 8) - bt * 8
                scc.inc(s.copy(g1T[:, bt * 8 * CH:(bt * 8 + n) * CH],
                               ps_tr[0:DH, 0:n * CH]))
            setm("ps_tr_free", scc)
            setm(f"g1T{bt}", scc)

        for t in range(nt):
            @blk.tensor
            def _(t_, t=t):
                wge(t_, f"g1T{t // 8}")
                if t == 0:
                    wge(t_, f"oev{nt - 1}")
                wge(t_, f"g2ev{t - 1}")
                t_.matmul(ps_o[:, 0:512], g1T[:, t * CH:(t + 1) * CH],
                          wg2b[:, 0:512], start=True, stop=True)
                ins = t_.matmul(ps_o[:, 512:1024], g1T[:, t * CH:(t + 1) * CH],
                                wg2b[:, 512:1024], start=True, stop=True)
                tc.inc(ins)
            setm(f"g2mm{t}", tc)

            @blk.scalar
            def _(s, t=t):
                mm = marks[f"g2mm{t}"]
                s.wait_ge(mm[0].h, mm[1])
                wge(s, f"g2spill{t - 2}")
                if t < 2:
                    wge(s, f"ospill{nt - 1}")
                    wge(s, f"ospill{nt - 2}")
                scc.inc(s.activation(osb[:, (t % 2) * E:(t % 2 + 1) * E], ps_o[:, :],
                                     AF.Sigmoid))
            setm(f"g2ev{t}", scc)

            @blk.scalar
            def _(s, t=t):
                wge(s, f"g2ev{t}")
                marks[f"g2spill{t}"] = P_g2.issue(s, lambda: s.dma_start(
                    out=g2_d[t], in_=osb[:, (t % 2) * E:(t % 2 + 1) * E]))
        setm("p4_t", tc)

        if max_phase <= 4:
            emit_final()
            return nc

        @blk.sync
        def _(y):
            wge(y, f"omm{nt - 1}")
            wge(y, f"scan{nt - 1}")
            wge(y, "ag_w")
            first = True
            for it in range(22):
                dst = (Q2[:, it * E:(it + 1) * E] if it < 11 else
                       C_wd[:, (it - 11) * E:(it - 10) * E])
                for r in range(NCORE):
                    S_wd.issue(y, lambda: y.dma_start(out=dst[:, r * CH:(r + 1) * CH],
                                              in_=agd_o[r, it * CH:(it + 1) * CH, :]),
                               fence=first)
                    first = False
        marks["wd_ld"] = S_wd.mark()

        for t in range(nt):
            slot = C_hqld[t % 2]

            @blk.sync
            def _(y, t=t, slot=slot):
                wge(y, f"spill{t}_1")
                wge(y, f"dcons{t - 2}")
                if t < 2:
                    wge(y, "wd_ld")
                marks[f"hqld{t}"] = S_hq.issue(y, lambda: y.dma_start(out=slot, in_=hq_d[t]))

            @blk.tensor
            def _(t_, t=t, slot=slot):
                wge(t_, f"hqld{t}")
                if t == 0:
                    wge(t_, "wd_ld")
                    wge(t_, "p4_t")
                    wge(t_, f"g2ev{nt - 1}")
                wge(t_, f"gev{t - 1}")
                for ns in range(2):
                    for it in range(22):
                        wsrc = (Q2[:, it * E + ns * 512:it * E + ns * 512 + 512]
                                if it < 11 else
                                C_wd[:, (it - 11) * E + ns * 512:
                                     (it - 11) * E + ns * 512 + 512])
                        ins = t_.matmul(ps_o[:, ns * 512:(ns + 1) * 512],
                                        slot[:, it * CH:(it + 1) * CH], wsrc,
                                        start=(it == 0), stop=(it == 21))
                tc.inc(ins)
            setm(f"dmm{t}", tc)
            setm(f"dcons{t}", tc)

            @blk.scalar
            def _(s, t=t):
                mm = marks[f"dmm{t}"]
                s.wait_ge(mm[0].h, mm[1])
                scc.inc(s.activation(Q3[:, t * E:(t + 1) * E], ps_o[:, :], AF.Silu))
            setm(f"gev{t}", scc)
        setm("g_done", scc)

        if max_phase <= 5:
            emit_final()
            return nc

        @blk.gpsimd
        def _(g):
            wge(g, "g_done")
            for et in range(8):
                G_wo.issue(g, lambda: g.dma_start(out=XN_wo[:, et * E:(et + 1) * E],
                                          in_=wo_in[et * CH:(et + 1) * CH, :]),
                           fence=(et == 0))
        marks["wo_ld"] = G_wo.mark()

        @blk.sync
        def _(y):
            wge(y, "g_done")
            S_p5c.issue(y, lambda: y.dma_start(out=C_gam, in_=gam_in))
            S_p5c.issue(y, lambda: y.dma_start(out=C_bet, in_=bet_in), fence=False)
            S_p5c.issue(y, lambda: y.dma_start(out=C_wful, in_=wful_in), fence=False)
        marks["p5c"] = S_p5c.mark()

        @blk.vector
        def _(v):
            wge(v, "p5c")
            vc.inc(v.memset(sin2[:, :], 0.0))
        setm("sin0", vc)

        for r in range(NCORE):
            @blk.sync
            def _(y, r=r):
                wge(y, "ag_t")
                wge(y, "g_done")
                wge(y, f"tcons{r - 2}")
                marks[f"tld{r}"] = S_tld.issue(y, lambda: y.dma_start(out=Q2_ts[r % 2],
                                                              in_=agt_o[r]))

            @blk.vector
            def _(v, r=r):
                wge(v, f"tld{r}")
                wge(v, "sin0")
                v.wait_ge(vc.h, vc.n)
                vc.inc(v.tensor_tensor(C_o[:, 0:512], Q2_ts[r % 2],
                                       C_wful[:, r * 512:(r + 1) * 512], ALU.mult))
                v.wait_ge(vc.h, vc.n)
                vc.inc(v.tensor_tensor(sin2[:, :], sin2[:, :], C_o[:, 0:512],
                                       ALU.add))
            setm(f"tcons{r}", vc)

        @blk.vector
        def _(v):
            wge(v, f"tcons{NCORE - 1}")
            vc.inc(v.tensor_copy(sj2[:, :], sin2[:, :]))
        setm("sdec-1", vc)

        pending5 = []

        def flush5():
            while pending5:
                pending5.pop(0)()

        for jc in range(nt):
            @blk.sync
            def _(y, jc=jc):
                wge(y, f"ocons{jc - 2}")
                if jc < 2:
                    wge(y, "g_done")
                wge(y, f"ospill{jc}")
                wge(y, f"g2spill{jc}")
                S_p5.issue(y, lambda: y.dma_start(out=Q2_op[jc % 2], in_=op_d[jc]))
                S_p5.issue(y, lambda: y.dma_start(out=Q2_g2[jc % 2], in_=g2_d[jc]),
                           fence=False)
                S_p5.issue(y, lambda: y.dma_start(out=C_x5[jc % 2],
                                          in_=x_in[jc * CH:(jc + 1) * CH, :]),
                           fence=False)
            marks[f"p5ld{jc}"] = S_p5.mark()

            @blk.vector
            def _(v, jc=jc):
                wge(v, f"sdec{jc - 1}")
                wge(v, f"oScons{jc - 1}")
                vc.inc(v.tensor_copy(sjb[:, :], sj2[:, :]))
                v.wait_ge(vc.h, vc.n)
                for et in range(8):
                    ins = v.tensor_tensor(
                        Q2_qch[:, et * CH:(et + 1) * CH],
                        Q1[:, et * SEG + jc * CH:et * SEG + (jc + 1) * CH],
                        lamqb[:, et * CH:(et + 1) * CH], ALU.mult)
                vc.inc(ins)
            setm(f"qch5_{jc}", vc)

            @blk.vector
            def _(v, jc=jc):
                wge(v, f"qch5_{jc}")
                for h in range(H):
                    et, mm_ = h // 2, h % 2
                    ins = v.tensor_scalar(
                        sj2[64 * mm_:64 * mm_ + 64, et * 64:(et + 1) * 64],
                        sj2[64 * mm_:64 * mm_ + 64, et * 64:(et + 1) * 64],
                        lam_c[h], None, ALU.mult)
                vc.inc(ins)
            setm(f"sdec{jc}", vc)

            @blk.tensor
            def _(t_, jc=jc):
                wge(t_, f"qch5_{jc}")
                if jc == 0:
                    wge(t_, f"gev{nt - 1}")
                wge(t_, f"ocomb{jc - 1}")
                for par in range(2):
                    for h in [hh for hh in range(H) if hh % 2 == par]:
                        et, mm_ = h // 2, h % 2
                        ins = t_.matmul(
                            ps_o[:, h * DH:(h + 1) * DH],
                            Q2_qch[64 * mm_:64 * mm_ + 64, et * CH:(et + 1) * CH],
                            sjb[64 * mm_:64 * mm_ + 64, et * 64:(et + 1) * 64],
                            start=True, stop=True)
                    tc.inc(ins)
                    t_.wait_ge(tc.h, tc.n)
            setm(f"oSmm{jc}", tc)
            setm(f"oScons{jc}", tc)
            flush5()

            @blk.vector
            def _(v, jc=jc):
                wge(v, f"oSmm{jc}")
                wge(v, f"p5ld{jc}")
                vc.inc(v.tensor_tensor(C_o, ps_o[:, :], Q2_op[jc % 2], ALU.add))
            setm(f"ocomb{jc}", vc)

            @blk.vector
            def _(v, jc=jc):
                wge(v, f"ocomb{jc}")
                vc.inc(v.bn_stats(stb[:, 0:6], C_o[:, 0:512]))
                vc.inc(v.bn_stats(stb[:, 6:12], C_o[:, 512:1024]))
                v.wait_ge(vc.h, vc.n)
                vc.inc(v.bn_aggr(stb[:, 12:14], stb[:, 0:12]))
                v.wait_ge(vc.h, vc.n)
                vc.inc(v.tensor_scalar(stb[:, 14:15], stb[:, 13:14], EPS, None, ALU.add))
            setm(f"var5_{jc}", vc)

            @blk.scalar
            def _(s, jc=jc):
                wge(s, f"var5_{jc}")
                scc.inc(s.activation(stb[:, 15:16], stb[:, 14:15], AF.Sqrt))
            setm(f"sqr5_{jc}", scc)

            @blk.vector
            def _(v, jc=jc):
                wge(v, f"sqr5_{jc}")
                wge(v, "p5c")
                vc.inc(v.reciprocal(stb[:, 16:17], stb[:, 15:16]))
                v.wait_ge(vc.h, vc.n)
                vc.inc(v.tensor_scalar(C_on, C_o, stb[:, 12:13], stb[:, 16:17],
                                       ALU.subtract, ALU.mult))
                v.wait_ge(vc.h, vc.n)
                vc.inc(v.tensor_tensor(C_og, C_on, C_gam, ALU.mult))
                v.wait_ge(vc.h, vc.n)
                vc.inc(v.tensor_tensor(C_on, C_og, C_bet, ALU.add))
                v.wait_ge(vc.h, vc.n)
                wge(v, f"ogT_tr{jc - 1}")
                vc.inc(v.tensor_tensor(Q2_ogtok, C_on, Q2_g2[jc % 2], ALU.mult))
            setm(f"og{jc}", vc)

            def mk_tail(jc=jc):
                @blk.tensor
                def _(t_, jc=jc):
                    wge(t_, f"og{jc}")
                    wge(t_, "ps_tr_free")
                    for et in range(8):
                        ins = t_.transpose(ps_tr[:, et * CH:(et + 1) * CH],
                                           Q2_ogtok[:, et * CH:(et + 1) * CH],
                                           idn[:, :])
                    tc.inc(ins)
                setm(f"ogT_tr{jc}", tc)

                @blk.scalar
                def _(s, jc=jc):
                    wge(s, f"ogT_tr{jc}")
                    wge(s, f"smm{jc - 1}")
                    scc.inc(s.copy(Q2_ogT, ps_tr[:, 0:E]))
                setm("ps_tr_free", scc)
                setm(f"ogT{jc}", scc)

                @blk.tensor
                def _(t_, jc=jc):
                    wge(t_, f"ogT{jc}")
                    wge(t_, "wo_ld")
                    wge(t_, f"fin{jc - 1}")
                    for ns in range(2):
                        for et in range(8):
                            ins = t_.matmul(
                                ps_sc[:, ns * 512:(ns + 1) * 512],
                                Q2_ogT[:, et * CH:(et + 1) * CH],
                                XN_wo[:, et * E + ns * 512:et * E + ns * 512 + 512],
                                start=(et == 0), stop=(et == 7))
                    tc.inc(ins)
                setm(f"smm{jc}", tc)

                @blk.vector
                def _(v, jc=jc):
                    wge(v, f"smm{jc}")
                    wge(v, f"outst{jc - 2}")
                    vc.inc(v.tensor_tensor(C_og, ps_sc[:, :],
                                           Q3[:, jc * E:(jc + 1) * E],
                                           ALU.mult))
                    v.wait_ge(vc.h, vc.n)
                    vc.inc(v.tensor_tensor(XN_out[jc % 2], C_og, C_x5[jc % 2],
                                           ALU.add))
                setm(f"fin{jc}", vc)
                setm(f"ocons{jc}", vc)

                @blk.sync
                def _(y, jc=jc):
                    wge(y, f"fin{jc}")
                    marks[f"outst{jc}"] = S_out.issue(y, lambda: y.dma_start(
                        out=out_d[jc * CH:(jc + 1) * CH, :], in_=XN_out[jc % 2]))
            pending5.append(mk_tail)

        flush5()
        emit_final()

    return nc


def _in_maps(inputs, nt):
    SEG = nt * CH
    x = np.asarray(inputs["L0_attn_x"], np.float32)
    masks, lamq, lamk = _host_consts()
    ident = np.eye(CH, dtype=np.float32).astype(ml_dtypes.bfloat16)
    gam = np.ascontiguousarray(np.broadcast_to(
        np.asarray(inputs["L0_attn_gamma"], np.float32)[None, :], (CH, E)))
    bet = np.ascontiguousarray(np.broadcast_to(
        np.asarray(inputs["L0_attn_beta"], np.float32)[None, :], (CH, E)))
    wg = np.asarray(inputs["L0_ffn_w_gate"], np.float32)
    wu = np.asarray(inputs["L0_ffn_w_up"], np.float32)
    wd = np.asarray(inputs["L0_ffn_w_down"], np.float32)
    common = {
        "wq": np.asarray(inputs["L0_attn_Wq"], np.float32),
        "wk": np.asarray(inputs["L0_attn_Wk"], np.float32),
        "wv": np.asarray(inputs["L0_attn_Wv"], np.float32),
        "wo": np.asarray(inputs["L0_attn_Wo"], np.float32),
        "wg1": np.asarray(inputs["L0_attn_Wg1"], np.float32),
        "wg2": np.asarray(inputs["L0_attn_Wg2"], np.float32),
        "gam": gam, "bet": bet, "ident": ident,
        "masks": masks, "lamq": lamq, "lamk": lamk,
    }
    maps = []
    for c in range(NCORE):
        sc, bc = c % 4, c // 4
        m = dict(common)
        m["x"] = np.ascontiguousarray(x[bc, sc * SEG:(sc + 1) * SEG, :])
        m["wgs"] = np.ascontiguousarray(wg[c * WSH:(c + 1) * WSH, :])
        m["wus"] = np.ascontiguousarray(wu[c * WSH:(c + 1) * WSH, :])
        m["wds"] = np.ascontiguousarray(wd[c * WDS:(c + 1) * WDS, :])
        m["wful"] = _host_wful(c, SEG)
        maps.append(m)
    return maps


def kernel(**inputs):
    nt = 16
    nc = build_program(nt)
    maps = _in_maps(inputs, nt)
    res = run_bass_kernel_spmd(nc, maps, core_ids=list(range(NCORE)))
    SEG = nt * CH
    out = np.empty((2, 4 * SEG, E), np.float32)
    for c in range(NCORE):
        sc, bc = c % 4, c // 4
        out[bc, sc * SEG:(sc + 1) * SEG, :] = np.asarray(res.results[c]["out"])
    return out
